# version 01
# baseline (speedup 1.0000x reference)
import contextlib

import ml_dtypes
import numpy as np

import concourse.bass as bass
import concourse.mybir as mybir
from concourse.alu_op_type import AluOpType as Op
from concourse.bass_utils import run_bass_kernel_spmd

F32 = mybir.dt.float32
BF16 = mybir.dt.bfloat16
FP8 = mybir.dt.float8e4
AF = mybir.ActivationFunctionType
BF = ml_dtypes.bfloat16

D = 2048
S = 2048
B = 4
NCORE = 8
SCALE = 0.08838834764831843


class Ctx:

    def __init__(self, nc, es):
        self.nc = nc
        self.es = es
        self.sems = {}

    def sem(self, name):
        if name not in self.sems:
            self.sems[name] = [self.es.enter_context(self.nc.semaphore(name)), 0]
        return self.sems[name]

    def inc(self, name, instr, k=1):
        s = self.sem(name)
        instr.then_inc(s[0], k)
        s[1] += k
        return s[1]

    def val(self, name):
        return self.sem(name)[1]

    def wait(self, eng, name, v=None):
        s = self.sem(name)
        eng.wait_ge(s[0], s[1] if v is None else v)

    def sb(self, es, name, shape, dtype):
        return es.enter_context(self.nc.sbuf_tensor(name, shape, dtype))

    def ps(self, es, name, shape):
        return es.enter_context(self.nc.psum_tensor(name, shape, F32))


def build_l0attn():
    nc = bass.Bass("TRN2", target_bir_lowering=False)
    xT = nc.dram_tensor("xT", [D, S], F32, kind="ExternalInput").ap()
    wq = nc.dram_tensor("wq", [D, 1024], F32, kind="ExternalInput").ap()
    wk = nc.dram_tensor("wk", [D, 1024], F32, kind="ExternalInput").ap()
    wv = nc.dram_tensor("wv", [D, 1024], F32, kind="ExternalInput").ap()
    wo = nc.dram_tensor("wo", [1024, D], F32, kind="ExternalInput").ap()
    gwT = nc.dram_tensor("gwT", [128, 128], F32, kind="ExternalInput").ap()
    gb = nc.dram_tensor("gb", [128, 1], F32, kind="ExternalInput").ap()
    base = nc.dram_tensor("base", [128, 1024], BF16, kind="ExternalInput").ap()
    partT = nc.dram_tensor("partT", [D, S], F32, kind="ExternalOutput").ap()
    dbgP = nc.dram_tensor("dbgP", [128, 2048], BF16, kind="ExternalOutput").ap()
    dbgS = nc.dram_tensor("dbgS", [1, 512], F32, kind="ExternalOutput").ap()
    dbgA = nc.dram_tensor("dbgA", [128, 512], F32, kind="ExternalOutput").ap()
    dbgG = nc.dram_tensor("dbgG", [128, 512], BF16, kind="ExternalOutput").ap()
    dbgK = nc.dram_tensor("dbgK", [128, 2048], BF16, kind="ExternalOutput").ap()
    dbgQ = nc.dram_tensor("dbgQ", [128, 2048], BF16, kind="ExternalOutput").ap()

    with contextlib.ExitStack() as es0:
        ctx = Ctx(nc, es0)
        emit_l0attn(ctx, es0, xT, wq, wk, wv, wo, gwT, gb, base, partT, dbgP, dbgS, dbgA, dbgG, dbgK, dbgQ)
    return nc


def emit_l0attn(ctx, es0, xT, wq, wk, wv, wo, gwT, gb, base, partT, dbgP, dbgS, dbgA, dbgG, dbgK, dbgQ):
    nc = ctx.nc
    block = es0.enter_context(nc.Block())

    oin = ctx.sb(es0, "oin", [128, 8 * S], BF16)
    basT = ctx.sb(es0, "basT", [128, 1024], BF16)
    gwb = ctx.sb(es0, "gwb", [128, 128], BF16)
    gwf = ctx.sb(es0, "gwf", [128, 128], F32)
    gbv = ctx.sb(es0, "gbv", [128, 1], F32)
    ones = ctx.sb(es0, "ones", [128, 1], BF16)
    onesf = ctx.sb(es0, "onesf", [1, 128], F32)

    es_ab = es0.enter_context(contextlib.ExitStack())
    qsb = ctx.sb(es_ab, "qsb", [128, 8 * S], BF16)
    ksb = ctx.sb(es_ab, "ksb", [128, 8 * S], BF16)
    vsb = ctx.sb(es_ab, "vsb", [128, 16 * 1024], BF16)

    es_a = es_ab.enter_context(contextlib.ExitStack())
    xbf = ctx.sb(es_a, "xbf", [128, 16 * 1024], BF16)
    xld = ctx.sb(es_a, "xld", [128, 1024], F32)
    wld = ctx.sb(es_a, "wld", [128, 16 * 128], F32)
    wbf = ctx.sb(es_a, "wbf", [128, 16 * 128], BF16)
    wvld = ctx.sb(es_a, "wvld", [128, 16 * 128], F32)
    wvbf = ctx.sb(es_a, "wvbf", [128, 16 * 128], BF16)
    pq = [ctx.ps(es_a, f"pq{i}", [128, 512]) for i in range(2)]

    @block.sync
    def _(sync):
        ctx.inc("ld", sync.dma_start(out=basT[:], in_=base[:, :]), 16)
        ctx.inc("ld", sync.dma_start(out=gwf[:], in_=gwT[:, :]), 16)
        ctx.inc("ld", sync.dma_start(out=gbv[:], in_=gb[:, :]), 16)

    @block.vector
    def _(v):
        ctx.wait(v, "ld")
        ctx.inc("cst", v.tensor_copy(gwb[:], gwf[:]))
        ctx.inc("cst", v.memset(ones[:], 1.0))
        ctx.inc("cst", v.memset(onesf[:], 1.0))

    for th in range(2):
        toff = th * 1024
        for c in range(16):
            @block.sync
            def _(sync, c=c, toff=toff):
                ctx.wait(sync, "cst")
                ctx.wait(sync, "pe", ctx.val("pe"))
                ctx.inc("ld", sync.dma_start(
                    out=xld[:], in_=xT[c * 128:(c + 1) * 128, toff:toff + 1024]), 16)

            @block.vector
            def _(v, c=c):
                ctx.wait(v, "ld")
                ctx.inc("cst", v.tensor_copy(xbf[:, c * 1024:(c + 1) * 1024], xld[:]))

        def proj_qk(wsrc, dst_sb):
            for oc in range(8):
                @block.sync
                def _(sync, oc=oc):
                    ctx.wait(sync, "pe", ctx.val("pe"))
                    src = wsrc[:, oc * 128:(oc + 1) * 128].rearrange("(c p) n -> p c n", p=128)
                    dst = wld[:].rearrange("p (c n) -> p c n", c=16)
                    ctx.inc("ld", sync.dma_start(out=dst, in_=src), 16)

                @block.vector
                def _(v):
                    ctx.wait(v, "ld")
                    ctx.inc("cst", v.tensor_copy(wbf[:], wld[:]))

                for q2 in range(2):
                    pb = pq[q2 % 2]
                    tloc = q2 * 512
                    @block.tensor
                    def _(t, oc=oc, q2=q2, pb=pb, tloc=tloc):
                        ctx.wait(t, "cst")
                        ctx.wait(t, "act", ctx.val("act"))
                        for c in range(16):
                            mm = t.matmul(pb[:], wbf[:, c * 128:(c + 1) * 128],
                                          xbf[:, c * 1024 + tloc: c * 1024 + tloc + 512],
                                          start=(c == 0), stop=(c == 15))
                        ctx.inc("pe", mm)

                    @block.scalar
                    def _(sc, oc=oc, pb=pb, dst_sb=dst_sb, tloc=tloc, toff=toff):
                        ctx.wait(sc, "pe")
                        o = oc * S + toff + tloc
                        ctx.inc("act", sc.copy(dst_sb[:, o:o + 512], pb[:]))

        proj_qk(wq, qsb)
        proj_qk(wk, ksb)

        for ft in range(8):
            @block.sync
            def _(sync, ft=ft):
                ctx.wait(sync, "pe", ctx.val("pe"))
                src = wv[:, ft * 128:(ft + 1) * 128].rearrange("(c p) n -> p c n", p=128)
                dst = wvld[:].rearrange("p (c n) -> p c n", c=16)
                ctx.inc("ld", sync.dma_start(out=dst, in_=src), 16)

            @block.vector
            def _(v):
                ctx.wait(v, "ld")
                ctx.inc("cst", v.tensor_copy(wvbf[:], wvld[:]))

            for tc in range(8):
                pb = pq[tc % 2]
                @block.tensor
                def _(t, ft=ft, tc=tc, pb=pb):
                    ctx.wait(t, "cst")
                    ctx.wait(t, "act", ctx.val("act"))
                    for c in range(16):
                        mm = t.matmul(pb[:, 0:128],
                                      xbf[:, c * 1024 + tc * 128: c * 1024 + tc * 128 + 128],
                                      wvbf[:, c * 128:(c + 1) * 128],
                                      start=(c == 0), stop=(c == 15))
                    ctx.inc("pe", mm)

                @block.scalar
                def _(sc, ft=ft, tc=tc, pb=pb, toff=toff):
                    ctx.wait(sc, "pe")
                    o = (toff // 128 + tc) * 1024 + ft * 128
                    ctx.inc("act", sc.copy(vsb[:, o:o + 128], pb[:, 0:128]))

    @block.tensor
    def _(t):
        ctx.wait(t, "act")
    @block.vector
    def _(v):
        ctx.wait(v, "act")
    @block.sync
    def _(sync):
        ctx.wait(sync, "act")
    @block.scalar
    def _(sc):
        ctx.wait(sc, "act")
    @block.gpsimd
    def _(g):
        ctx.wait(g, "act")

    es_a.close()

    es_b = es_ab.enter_context(contextlib.ExitStack())
    psb = [ctx.sb(es_b, f"psb{i}", [128, 16 * 512], BF16) for i in range(2)]
    ptmp = [ctx.sb(es_b, f"ptmp{i}", [128, 512], BF16) for i in range(2)]
    gt = ctx.sb(es_b, "gt", [128, 512], BF16)
    t1 = ctx.sb(es_b, "t1", [128, 512], F32)
    rnb = ctx.sb(es_b, "rnb", [128, 512], F32)
    lnrow = ctx.sb(es_b, "lnrow", [1, 512], F32)
    rnrow = ctx.sb(es_b, "rnrow", [1, 512], F32)
    psc = [ctx.ps(es_b, f"psc{i}", [128, 512]) for i in range(2)]
    pav = [ctx.ps(es_b, f"pav{i}", [128, 512]) for i in range(2)]
    psm = [ctx.ps(es_b, f"psm{i}", [1, 512]) for i in range(2)]
    pg = ctx.ps(es_b, "pg", [128, 512])
    prn = ctx.ps(es_b, "prn", [128, 512])

    it = 0
    for h in range(8):
        for qt in range(4):
            qoff = qt * 512
            nvis = qt * 4 + 4
            pb = psb[it % 2]
            av_ps = pav[it % 2]
            sm_ps = psm[it % 2]

            @block.tensor
            def _(t, h=h, qoff=qoff):
                ctx.wait(t, "dve", ctx.val("dve"))
                ctx.inc("pe", t.matmul(pg[:], gwb[:], qsb[:, h * S + qoff: h * S + qoff + 512],
                                       start=True, stop=True))

            @block.scalar
            def _(sc):
                ctx.wait(sc, "pe")
                ctx.inc("act", sc.activation(gt[:], pg[:], AF.Sigmoid, bias=gbv[:, 0:1]))

            for c in range(nvis):
                masked = (c * 128 + 127 > qoff)
                sc_ps = psc[c % 2]
                @block.tensor
                def _(t, h=h, qoff=qoff, c=c, pb=pb, sc_ps=sc_ps):
                    if c >= 2:
                        ctx.wait(t, "act", ctx.val("act"))
                        ctx.wait(t, "cst", ctx.val("cst"))
                    else:
                        ctx.wait(t, "dve", ctx.val("dve"))
                    ctx.inc("pe", t.matmul(sc_ps[:], ksb[:, h * S + c * 128: h * S + c * 128 + 128],
                                           qsb[:, h * S + qoff: h * S + qoff + 512],
                                           start=True, stop=True))

                if masked:
                    u0 = qoff - 128 * c + 512
                    pt = ptmp[c % 2]
                    @block.scalar
                    def _(sc, pt=pt, sc_ps=sc_ps):
                        ctx.wait(sc, "pe")
                        ctx.inc("act", sc.activation(pt[:], sc_ps[:], AF.Exp, scale=SCALE))

                    @block.vector
                    def _(v, c=c, pb=pb, pt=pt, u0=u0):
                        ctx.wait(v, "act")
                        ctx.inc("cst", v.tensor_tensor(pb[:, c * 512:(c + 1) * 512],
                                                       pt[:], basT[:, u0:u0 + 512], Op.mult))
                else:
                    @block.scalar
                    def _(sc, c=c, pb=pb, sc_ps=sc_ps):
                        ctx.wait(sc, "pe")
                        ctx.inc("act", sc.activation(pb[:, c * 512:(c + 1) * 512], sc_ps[:],
                                                     AF.Exp, scale=SCALE))

            @block.tensor
            def _(t, h=h, nvis=nvis, pb=pb, av_ps=av_ps, sm_ps=sm_ps):
                ctx.wait(t, "act", ctx.val("act"))
                ctx.wait(t, "cst", ctx.val("cst"))
                for c in range(nvis):
                    mm = t.matmul(av_ps[:], vsb[:, c * 1024 + h * 128: c * 1024 + h * 128 + 128],
                                  pb[:, c * 512:(c + 1) * 512], start=(c == 0), stop=(c == nvis - 1))
                ctx.inc("pe", mm)
                for c in range(nvis):
                    mm = t.matmul(sm_ps[:], ones[:, 0:1], pb[:, c * 512:(c + 1) * 512],
                                  start=(c == 0), stop=(c == nvis - 1))
                ctx.inc("pe", mm)

            @block.scalar
            def _(sc, sm_ps=sm_ps):
                ctx.wait(sc, "pe")
                ctx.inc("act", sc.activation(lnrow[:], sm_ps[:], AF.Ln))

            @block.scalar
            def _(sc):
                ctx.wait(sc, "act")
                ctx.inc("act", sc.activation(rnrow[:], lnrow[:], AF.Exp, scale=-1.0))

            @block.tensor
            def _(t):
                ctx.wait(t, "act")
                ctx.inc("pe", t.matmul(prn[:], onesf[:], rnrow[:], start=True, stop=True))

            @block.vector
            def _(v, h=h, qoff=qoff, av_ps=av_ps):
                ctx.wait(v, "pe")
                ctx.inc("dve", v.tensor_copy(rnb[:], prn[:]))
                ctx.wait(v, "dve")
                ctx.inc("dve", v.tensor_tensor(t1[:], av_ps[:], rnb[:], Op.mult))
                ctx.wait(v, "dve")
                ctx.inc("dve", v.tensor_tensor(oin[:, h * S + qoff: h * S + qoff + 512],
                                               t1[:], gt[:], Op.mult))

            if it == 0 and dbgP is not None:
                @block.vector
                def _(v, pb=pb, av_ps=av_ps, sm_ps=sm_ps):
                    ctx.wait(v, "dve")
                    ctx.inc("dve", v.tensor_copy(t1[:], av_ps[:]))
                    ctx.wait(v, "dve")
                    ctx.inc("dve", v.tensor_copy(rnb[0:1, :], sm_ps[:]))

                @block.gpsimd
                def _(g, pb=pb, sm_ps=sm_ps):
                    ctx.wait(g, "dve")
                    ctx.inc("sto", g.dma_start(out=dbgP[:, :], in_=pb[:, 0:2048]), 16)
                    ctx.inc("sto", g.dma_start(out=dbgA[:, :], in_=t1[:]), 16)
                    ctx.inc("sto", g.dma_start(out=dbgG[:, :], in_=gt[:]), 16)
                    ctx.inc("sto", g.dma_start(out=dbgK[:, :], in_=ksb[:, 0:2048]), 16)
                    ctx.inc("sto", g.dma_start(out=dbgQ[:, :], in_=qsb[:, 0:2048]), 16)

                @block.scalar
                def _(sc, sm_ps=sm_ps):
                    ctx.wait(sc, "dve")
                    ctx.inc("sto", sc.dma_start(out=dbgS[:, :], in_=rnb[0:1, :]), 16)
                @block.tensor
                def _(t):
                    ctx.wait(t, "sto")
            it += 1

    for engn in ("tensor", "vector", "scalar", "sync", "gpsimd"):
        @getattr(block, engn)
        def _(e):
            ctx.wait(e, "dve")

    es_b.close()
    es_ab.close()

    es_c = es0.enter_context(contextlib.ExitStack())
    wold = ctx.sb(es_c, "wold", [128, 8 * S], F32)
    wobf = ctx.sb(es_c, "wobf", [128, 8 * S], BF16)
    stg = [ctx.sb(es_c, f"stg{i}", [128, 512], F32) for i in range(4)]
    po = [ctx.ps(es_c, f"po{i}", [128, 512]) for i in range(2)]

    @block.sync
    def _(sync):
        src = wo.rearrange("(c p) n -> p c n", p=128)
        dst = wold[:].rearrange("p (c n) -> p c n", c=8)
        ctx.inc("ld", sync.dma_start(out=dst, in_=src), 16)

    @block.vector
    def _(v):
        ctx.wait(v, "ld")
        ctx.inc("cst", v.tensor_copy(wobf[:], wold[:]))

    si = 0
    for oc2 in range(16):
        for t4 in range(4):
            pb = po[si % 2]
            sb = stg[si % 4]
            @block.tensor
            def _(t, oc2=oc2, t4=t4, pb=pb, si=si):
                ctx.wait(t, "cst")
                if si >= 2:
                    ctx.wait(t, "dve", ctx.val("dve"))
                for c in range(8):
                    mm = t.matmul(pb[:], wobf[:, c * S + oc2 * 128: c * S + oc2 * 128 + 128],
                                  oin[:, c * S + t4 * 512: c * S + t4 * 512 + 512],
                                  start=(c == 0), stop=(c == 7))
                ctx.inc("pe", mm)

            @block.vector
            def _(v, pb=pb, sb=sb, si=si):
                ctx.wait(v, "pe")
                if si >= 4:
                    ctx.wait(v, "sto", (si - 3) * 16)
                ctx.inc("dve", v.tensor_copy(sb[:], pb[:]))

            @block.gpsimd
            def _(g, oc2=oc2, t4=t4, sb=sb, si=si):
                ctx.wait(g, "dve")
                ctx.inc("sto", g.dma_start(
                    out=partT[oc2 * 128:(oc2 + 1) * 128, t4 * 512:(t4 + 1) * 512], in_=sb[:]), 16)
            si += 1

    @block.gpsimd
    def _(g):
        ctx.wait(g, "sto")

    es_c.close()


def _mkbase():
    u = np.arange(1024)
    p = np.arange(128)
    return (p[:, None] <= (u[None, :] - 512)).astype(BF)


_L0ATTN_NC = None


def run_l0attn(inputs):
    global _L0ATTN_NC
    x = np.asarray(inputs["L0_attn_x"], dtype=np.float32)
    qw = np.asarray(inputs["L0_attn_q_w"], dtype=np.float32)
    kw = np.asarray(inputs["L0_attn_k_w"], dtype=np.float32)
    vw = np.asarray(inputs["L0_attn_v_w"], dtype=np.float32)
    ow = np.asarray(inputs["L0_attn_o_w"], dtype=np.float32)
    gw = np.asarray(inputs["L0_attn_gate_w"], dtype=np.float32)
    gb = np.asarray(inputs["L0_attn_gate_b"], dtype=np.float32)
    base = _mkbase()
    qwT = np.ascontiguousarray(qw.T)
    kwT = np.ascontiguousarray(kw.T)
    vwT = np.ascontiguousarray(vw.T)
    owT = np.ascontiguousarray(ow.T)
    gwT = np.ascontiguousarray(gw.T)
    in_maps = []
    for c in range(NCORE):
        b, hb = c // 2, c % 2
        in_maps.append({
            "xT": np.ascontiguousarray(x[b].T),
            "wq": np.ascontiguousarray(qwT[:, hb * 1024:(hb + 1) * 1024]),
            "wk": np.ascontiguousarray(kwT[:, hb * 1024:(hb + 1) * 1024]),
            "wv": np.ascontiguousarray(vwT[:, hb * 1024:(hb + 1) * 1024]),
            "wo": np.ascontiguousarray(owT[hb * 1024:(hb + 1) * 1024, :]),
            "gwT": gwT,
            "gb": gb.reshape(128, 1),
            "base": base,
        })
    if _L0ATTN_NC is None:
        _L0ATTN_NC = build_l0attn()
    res = run_bass_kernel_spmd(_L0ATTN_NC, in_maps, core_ids=list(range(NCORE)))
    return [res.results[c]["partT"] for c in range(NCORE)]




def build_l0ffn():
    nc = bass.Bass("TRN2", target_bir_lowering=False)
    xTt = nc.dram_tensor("xTt", [D, 1024], F32, kind="ExternalInput").ap()
    p0T = nc.dram_tensor("p0T", [D, 1024], F32, kind="ExternalInput").ap()
    p1T = nc.dram_tensor("p1T", [D, 1024], F32, kind="ExternalInput").ap()
    wg = nc.dram_tensor("wg", [D, 6144], F32, kind="ExternalInput").ap()
    wu = nc.dram_tensor("wu", [D, 6144], F32, kind="ExternalInput").ap()
    wd = nc.dram_tensor("wd", [6144, D], F32, kind="ExternalInput").ap()
    gsc = nc.dram_tensor("gsc", [6144, 1], F32, kind="ExternalInput").ap()
    usc = nc.dram_tensor("usc", [6144, 1], F32, kind="ExternalInput").ap()
    dsc = nc.dram_tensor("dsc", [D, 1], F32, kind="ExternalInput").ap()
    xln2T = nc.dram_tensor("xln2T", [D, 1024], F32, kind="ExternalOutput").ap()
    xl1d = nc.dram_tensor("xl1d", [D, 1024], F32).ap()
    xrd = nc.dram_tensor("xrd", [D, 1024], F32).ap()
    xnd = nc.dram_tensor("xnd", [D, 1024], F32).ap()

    with contextlib.ExitStack() as es0:
        ctx = Ctx(nc, es0)
        emit_l0ffn(ctx, es0, xTt, p0T, p1T, wg, wu, wd, gsc, usc, dsc, xln2T, xl1d, xrd, xnd)
    return nc


def emit_ln_calc(ctx, block, psum_sum, psum_sq, rows, eps, W=1024):
    mrow, vrow, rrow, epsrow = rows
    msq = rrow

    @block.vector
    def _(v):
        ctx.wait(v, "pe")
        ctx.inc("dve", v.tensor_scalar(mrow[:, 0:W], psum_sum[:], 1.0 / D, None, Op.mult))
        ctx.wait(v, "dve")
        ctx.inc("dve", v.tensor_tensor(msq[:, 0:W], mrow[:, 0:W], mrow[:, 0:W], Op.mult))
        ctx.wait(v, "dve")
        ctx.inc("dve", v.scalar_tensor_tensor(vrow[:, 0:W], psum_sq[:], 1.0 / D, msq[:, 0:W],
                                              Op.mult, Op.subtract))

    @block.scalar
    def _(sc):
        ctx.wait(sc, "dve")
        ctx.inc("act", sc.activation(rrow[:, 0:W], vrow[:, 0:W], AF.Ln, bias=epsrow[0:1, 0:1]))
        ctx.wait(sc, "act")
        ctx.inc("act", sc.activation(rrow[:, 0:W], rrow[:, 0:W], AF.Exp, scale=-0.5))


def emit_ln_bcast(ctx, block, rows, pmb, prb, onesf, W=1024):
    mrow, vrow, rrow = rows[:3]

    @block.tensor
    def _(t):
        ctx.wait(t, "act")
        for o in range(0, W, 512):
            t.matmul(pmb[:, o:o + 512], onesf[:], mrow[:, o:o + 512], start=True, stop=True)
            mm = t.matmul(prb[:, o:o + 512], onesf[:], rrow[:, o:o + 512], start=True, stop=True)
        ctx.inc("pe", mm)


def emit_l0ffn(ctx, es0, xTt, p0T, p1T, wg, wu, wd, gsc, usc, dsc, xln2T, xl1d, xrd, xnd):
    nc = ctx.nc
    block = es0.enter_context(nc.Block())
    NI = 48

    FUSE = ctx.sb(es0, "FUSE", [128, NI * 1024], BF16)
    onesb = ctx.sb(es0, "onesb", [128, 1], BF16)
    onesf = ctx.sb(es0, "onesf", [1, 128], F32)
    rows = tuple(ctx.sb(es0, f"row{i}", [1, 1024], F32) for i in range(3))
    epsrow = ctx.sb(es0, "epsrow", [1, 1], F32)
    rows = rows + (epsrow,)

    @block.vector
    def _(v):
        ctx.inc("cst", v.memset(onesb[:], 1.0))
        ctx.inc("cst", v.memset(onesf[:], 1.0))
        ctx.inc("cst", v.memset(epsrow[:], 1e-5))

    es_a = es0.enter_context(contextlib.ExitStack())
    XLB = ctx.sb(es_a, "XLB", [128, 16 * 1024], BF16)
    tsq = ctx.sb(es_a, "tsq", [128, 1024], BF16)
    tchunk = ctx.sb(es_a, "tchunk", [128, 1024], F32)
    tc2 = ctx.sb(es_a, "tc2", [128, 1024], F32)
    ldx = ctx.sb(es_a, "ldx", [128, 1024], F32)
    ldp = tc2

    es_ln = es_a.enter_context(contextlib.ExitStack())
    psum_s = ctx.ps(es_ln, "psum_s", [1, 1024])
    psum_q = ctx.ps(es_ln, "psum_q", [1, 1024])
    pmb = ctx.ps(es_ln, "pmb", [128, 1024])
    prb = ctx.ps(es_ln, "prb", [128, 1024])

    for c in range(16):
        @block.sync
        def _(sync, c=c):
            ctx.wait(sync, "dve", ctx.val("dve"))
            ctx.wait(sync, "sto", ctx.val("sto"))
            ctx.inc("ld", sync.dma_start(out=ldx[:], in_=xTt[c * 128:(c + 1) * 128, :]), 16)
            ctx.inc("ld", sync.dma_start(out=ldp[:], in_=p0T[c * 128:(c + 1) * 128, :]), 16)

        @block.vector
        def _(v, c=c):
            ctx.wait(v, "ld")
            ctx.inc("dve", v.tensor_tensor(tchunk[:], ldx[:], ldp[:], Op.add))

        @block.sync
        def _(sync, c=c):
            ctx.wait(sync, "dve")
            ctx.inc("ld", sync.dma_start(out=ldp[:], in_=p1T[c * 128:(c + 1) * 128, :]), 16)

        @block.vector
        def _(v, c=c):
            ctx.wait(v, "ld")
            ctx.inc("dve", v.tensor_tensor(ldx[:], tchunk[:], ldp[:], Op.add))
            ctx.wait(v, "dve")
            ctx.inc("dve", v.tensor_copy(XLB[:, c * 1024:(c + 1) * 1024], ldx[:]))

        @block.gpsimd
        def _(g, c=c):
            ctx.wait(g, "dve")
            ctx.inc("sto", g.dma_start(out=xrd[c * 128:(c + 1) * 128, :], in_=ldx[:]), 16)

        @block.scalar
        def _(sc, c=c):
            ctx.wait(sc, "dve")
            ctx.wait(sc, "pe", ctx.val("pe"))
            ctx.inc("act", sc.activation(tsq[:], XLB[:, c * 1024:(c + 1) * 1024], AF.Square))

        @block.tensor
        def _(t, c=c):
            ctx.wait(t, "act")
            t.matmul(psum_s[:, 0:512], onesb[:], XLB[:, c * 1024: c * 1024 + 512], start=(c == 0), stop=(c == 15))
            t.matmul(psum_s[:, 512:1024], onesb[:], XLB[:, c * 1024 + 512: c * 1024 + 1024], start=(c == 0), stop=(c == 15))
            t.matmul(psum_q[:, 0:512], onesb[:], tsq[:, 0:512], start=(c == 0), stop=(c == 15))
            mm = t.matmul(psum_q[:, 512:1024], onesb[:], tsq[:, 512:1024], start=(c == 0), stop=(c == 15))
            ctx.inc("pe", mm)

    emit_ln_calc(ctx, block, psum_s, psum_q, rows, 1e-5)
    emit_ln_bcast(ctx, block, rows, pmb, prb, onesf)

    for c in range(16):
        @block.sync
        def _(sync, c=c):
            ctx.wait(sync, "dve", ctx.val("dve"))
            ctx.wait(sync, "sto", ctx.val("sto"))
            ctx.inc("ld", sync.dma_start(out=ldx[:], in_=xrd[c * 128:(c + 1) * 128, :]), 16)

        @block.vector
        def _(v, c=c):
            ctx.wait(v, "ld")
            ctx.wait(v, "pe")
            ctx.inc("dve", v.tensor_tensor(tchunk[:], ldx[:], pmb[:], Op.subtract))
            ctx.wait(v, "dve")
            ctx.inc("dve", v.tensor_tensor(tc2[:], tchunk[:], prb[:], Op.mult))
            ctx.wait(v, "dve")
            ctx.inc("dve", v.tensor_copy(XLB[:, c * 1024:(c + 1) * 1024], tc2[:]))

        @block.gpsimd
        def _(g, c=c):
            ctx.wait(g, "dve")
            ctx.inc("sto", g.dma_start(out=xl1d[c * 128:(c + 1) * 128, :], in_=tc2[:]), 16)

    @block.tensor
    def _(t):
        ctx.wait(t, "dve")
    es_ln.close()

    wgl = ctx.sb(es_a, "wgl", [128, 16 * 128], F32)
    w8 = ctx.sb(es_a, "w8", [128, 16 * 128], FP8)
    wgb = ctx.sb(es_a, "wgb", [128, 16 * 128], BF16)
    wub = ctx.sb(es_a, "wub", [128, 16 * 128], BF16)
    gscv = ctx.sb(es_a, "gscv", [128, 1], F32)
    uscv = ctx.sb(es_a, "uscv", [128, 1], F32)
    sgt = ctx.sb(es_a, "sgt", [128, 512], BF16)
    upt = ctx.sb(es_a, "upt", [128, 512], BF16)
    pgu = [ctx.ps(es_a, f"pgu{i}", [128, 512]) for i in range(4)]

    for ic in range(NI):
        @block.sync
        def _(sync, ic=ic):
            ctx.wait(sync, "pe", ctx.val("pe"))
            srcg = wg[:, ic * 128:(ic + 1) * 128].rearrange("(c p) n -> p c n", p=128)
            ctx.inc("ld", sync.dma_start(out=wgl[:].rearrange("p (c n) -> p c n", c=16), in_=srcg), 16)
            ctx.inc("ld", sync.dma_start(out=gscv[:], in_=gsc[ic * 128:(ic + 1) * 128, :]), 16)
            ctx.inc("ld", sync.dma_start(out=uscv[:], in_=usc[ic * 128:(ic + 1) * 128, :]), 16)

        @block.vector
        def _(v, ic=ic):
            ctx.wait(v, "ld")
            ctx.inc("cst", v.tensor_copy(w8[:], wgl[:]))
            ctx.wait(v, "cst")
            ctx.inc("cst", v.tensor_copy(wgb[:], w8[:]))

        @block.sync
        def _(sync, ic=ic):
            ctx.wait(sync, "cst")
            srcu = wu[:, ic * 128:(ic + 1) * 128].rearrange("(c p) n -> p c n", p=128)
            ctx.inc("ld", sync.dma_start(out=wgl[:].rearrange("p (c n) -> p c n", c=16), in_=srcu), 16)

        @block.vector
        def _(v, ic=ic):
            ctx.wait(v, "ld")
            ctx.inc("cst", v.tensor_copy(w8[:], wgl[:]))
            ctx.wait(v, "cst")
            ctx.inc("cst", v.tensor_copy(wub[:], w8[:]))

        for t2 in range(2):
            pg_ = pgu[(t2 % 2) * 2]
            pu_ = pgu[(t2 % 2) * 2 + 1]
            @block.tensor
            def _(t, ic=ic, t2=t2, pg_=pg_, pu_=pu_):
                ctx.wait(t, "cst")
                ctx.wait(t, "dve", ctx.val("dve"))
                ctx.wait(t, "act", ctx.val("act"))
                for c in range(16):
                    t.matmul(pg_[:], wgb[:, c * 128:(c + 1) * 128],
                             XLB[:, c * 1024 + t2 * 512: c * 1024 + t2 * 512 + 512],
                             start=(c == 0), stop=(c == 15))
                for c in range(16):
                    mm = t.matmul(pu_[:], wub[:, c * 128:(c + 1) * 128],
                                  XLB[:, c * 1024 + t2 * 512: c * 1024 + t2 * 512 + 512],
                                  start=(c == 0), stop=(c == 15))
                ctx.inc("pe", mm)

            @block.scalar
            def _(sc, pg_=pg_):
                ctx.wait(sc, "pe")
                ctx.wait(sc, "dve", ctx.val("dve"))
                ctx.inc("act", sc.activation(sgt[:], pg_[:], AF.Silu, scale=gscv[:, 0:1]))

            @block.vector
            def _(v, ic=ic, t2=t2, pu_=pu_):
                ctx.wait(v, "act")
                ctx.inc("dve", v.tensor_scalar(upt[:], pu_[:], uscv[:, 0:1], None, Op.mult))
                ctx.wait(v, "dve")
                o = ic * 1024 + t2 * 512
                ctx.inc("dve", v.tensor_tensor(FUSE[:, o:o + 512], sgt[:], upt[:], Op.mult))

    for engn in ("tensor", "vector", "scalar", "sync", "gpsimd"):
        @getattr(block, engn)
        def _(e):
            ctx.wait(e, "dve")
            ctx.wait(e, "sto")
    es_a.close()

    es_c = es0.enter_context(contextlib.ExitStack())
    wdl = ctx.sb(es_c, "wdl", [128, NI * 128], F32)
    wd8 = ctx.sb(es_c, "wd8", [128, NI * 128], FP8)
    wdb = ctx.sb(es_c, "wdb", [128, NI * 128], BF16)
    dscv2 = ctx.sb(es_c, "dscv2", [128, 1], F32)
    xl1c = ctx.sb(es_c, "xl1c", [128, 1024], F32)
    td = ctx.sb(es_c, "td", [128, 1024], F32)
    td2 = ctx.sb(es_c, "td2", [128, 1024], F32)
    tbf2 = ctx.sb(es_c, "tbf2", [128, 1024], BF16)
    tsq2 = ctx.sb(es_c, "tsq2", [128, 1024], BF16)

    es_dn = es_c.enter_context(contextlib.ExitStack())
    pdn = [ctx.ps(es_dn, f"pdn{i}", [128, 512]) for i in range(2)]
    psum_s2 = ctx.ps(es_dn, "psum_s2", [1, 1024])
    psum_q2 = ctx.ps(es_dn, "psum_q2", [1, 1024])

    for oc in range(16):
        @block.sync
        def _(sync, oc=oc):
            ctx.wait(sync, "pe", ctx.val("pe"))
            src = wd[:, oc * 128:(oc + 1) * 128].rearrange("(c p) n -> p c n", p=128)
            ctx.inc("ld", sync.dma_start(out=wdl[:].rearrange("p (c n) -> p c n", c=NI), in_=src), 16)
            ctx.inc("ld", sync.dma_start(out=dscv2[:], in_=dsc[oc * 128:(oc + 1) * 128, :]), 16)
            ctx.wait(sync, "dve", ctx.val("dve"))
            ctx.inc("ld", sync.dma_start(out=xl1c[:], in_=xl1d[oc * 128:(oc + 1) * 128, :]), 16)

        @block.vector
        def _(v, oc=oc):
            ctx.wait(v, "ld")
            ctx.inc("cst", v.tensor_copy(wd8[:], wdl[:]))
            ctx.wait(v, "cst")
            ctx.inc("cst", v.tensor_copy(wdb[:], wd8[:]))

        for t2 in range(2):
            pb = pdn[t2]
            @block.tensor
            def _(t, oc=oc, t2=t2, pb=pb):
                ctx.wait(t, "cst")
                ctx.wait(t, "dve", ctx.val("dve"))
                for c in range(NI):
                    mm = t.matmul(pb[:], wdb[:, c * 128:(c + 1) * 128],
                                  FUSE[:, c * 1024 + t2 * 512: c * 1024 + t2 * 512 + 512],
                                  start=(c == 0), stop=(c == NI - 1))
                ctx.inc("pe", mm)

        @block.vector
        def _(v, oc=oc):
            ctx.wait(v, "pe")
            if oc > 0:
                ctx.wait(v, "sto", ctx.val("sto"))
            ctx.inc("dve", v.tensor_scalar(td[:, 0:512], pdn[0][:], dscv2[:, 0:1], None, Op.mult))
            ctx.inc("dve", v.tensor_scalar(td[:, 512:1024], pdn[1][:], dscv2[:, 0:1], None, Op.mult))
            ctx.wait(v, "dve")
            ctx.inc("dve", v.tensor_tensor(td2[:], td[:], xl1c[:], Op.add))
            ctx.wait(v, "dve")
            ctx.inc("dve", v.tensor_copy(tbf2[:], td2[:]))

        @block.gpsimd
        def _(g, oc=oc):
            ctx.wait(g, "dve")
            ctx.inc("sto", g.dma_start(out=xnd[oc * 128:(oc + 1) * 128, :], in_=td2[:]), 16)

        @block.scalar
        def _(sc, oc=oc):
            ctx.wait(sc, "dve")
            ctx.wait(sc, "pe", ctx.val("pe"))
            ctx.inc("act", sc.activation(tsq2[:], tbf2[:], AF.Square))

        @block.tensor
        def _(t, oc=oc):
            ctx.wait(t, "act")
            t.matmul(psum_s2[:, 0:512], onesb[:], tbf2[:, 0:512], start=(oc == 0), stop=(oc == 15))
            t.matmul(psum_s2[:, 512:1024], onesb[:], tbf2[:, 512:1024], start=(oc == 0), stop=(oc == 15))
            t.matmul(psum_q2[:, 0:512], onesb[:], tsq2[:, 0:512], start=(oc == 0), stop=(oc == 15))
            mm = t.matmul(psum_q2[:, 512:1024], onesb[:], tsq2[:, 512:1024], start=(oc == 0), stop=(oc == 15))
            ctx.inc("pe", mm)

    emit_ln_calc(ctx, block, psum_s2, psum_q2, rows, 1e-5)

    @block.tensor
    def _(t):
        ctx.wait(t, "act")
    es_dn.close()
    pmb2 = ctx.ps(es_c, "pmb2", [128, 1024])
    prb2 = ctx.ps(es_c, "prb2", [128, 1024])
    emit_ln_bcast(ctx, block, rows, pmb2, prb2, onesf)

    for c in range(16):
        @block.sync
        def _(sync, c=c):
            ctx.wait(sync, "dve", ctx.val("dve"))
            ctx.wait(sync, "sto", ctx.val("sto"))
            ctx.inc("ld", sync.dma_start(out=xl1c[:], in_=xnd[c * 128:(c + 1) * 128, :]), 16)

        @block.vector
        def _(v, c=c):
            ctx.wait(v, "ld")
            ctx.wait(v, "pe")
            ctx.inc("dve", v.tensor_tensor(td[:], xl1c[:], pmb2[:], Op.subtract))
            ctx.wait(v, "dve")
            ctx.inc("dve", v.tensor_tensor(td2[:], td[:], prb2[:], Op.mult))

        @block.gpsimd
        def _(g, c=c):
            ctx.wait(g, "dve")
            ctx.inc("sto", g.dma_start(out=xln2T[c * 128:(c + 1) * 128, :], in_=td2[:]), 16)

    @block.gpsimd
    def _(g):
        ctx.wait(g, "sto")
    es_c.close()


_L0FFN_NC = None


def run_l0ffn(inputs, parts):
    global _L0FFN_NC
    x = np.asarray(inputs["L0_attn_x"], dtype=np.float32)
    in_maps = []
    for c in range(NCORE):
        b, sl = c // 2, c % 2
        tsl = slice(sl * 1024, (sl + 1) * 1024)
        in_maps.append({
            "xTt": np.ascontiguousarray(x[b].T[:, tsl]),
            "p0T": np.ascontiguousarray(parts[2 * b][:, tsl]),
            "p1T": np.ascontiguousarray(parts[2 * b + 1][:, tsl]),
            "wg": np.asarray(inputs["L0_ffn_gate_w"], dtype=np.float32),
            "wu": np.asarray(inputs["L0_ffn_up_w"], dtype=np.float32),
            "wd": np.asarray(inputs["L0_ffn_down_w"], dtype=np.float32),
            "gsc": np.asarray(inputs["L0_ffn_gate_scale"], dtype=np.float32).reshape(-1, 1),
            "usc": np.asarray(inputs["L0_ffn_up_scale"], dtype=np.float32).reshape(-1, 1),
            "dsc": np.asarray(inputs["L0_ffn_down_scale"], dtype=np.float32).reshape(-1, 1),
        })
    if _L0FFN_NC is None:
        _L0FFN_NC = build_l0ffn()
    res = run_bass_kernel_spmd(_L0FFN_NC, in_maps, core_ids=list(range(NCORE)))
    return [res.results[c]["xln2T"] for c in range(NCORE)]




def build_l1attn():
    nc = bass.Bass("TRN2", target_bir_lowering=False)
    xlT = nc.dram_tensor("xlT", [D, S], F32, kind="ExternalInput").ap()
    wia = nc.dram_tensor("wia", [D, 1024], F32, kind="ExternalInput").ap()
    wib = nc.dram_tensor("wib", [D, 1024], F32, kind="ExternalInput").ap()
    ap_in = nc.dram_tensor("ap_in", [128, 1], F32, kind="ExternalInput").ap()
    hT = nc.dram_tensor("hT", [1024, S], F32, kind="ExternalOutput").ap()

    with contextlib.ExitStack() as es0:
        ctx = Ctx(nc, es0)
        emit_l1attn(ctx, es0, xlT, wia, wib, ap_in, hT)
    return nc


def emit_l1attn(ctx, es0, xlT, wia, wib, ap_in, hT):
    nc = ctx.nc
    block = es0.enter_context(nc.Block())

    xbf = ctx.sb(es0, "xbf", [128, 16 * S], BF16)
    xld = ctx.sb(es0, "xld", [128, S], F32)
    apv = ctx.sb(es0, "apv", [128, 1], F32)
    onescol = ctx.sb(es0, "onescol", [128, 1], F32)
    zcol = ctx.sb(es0, "zcol", [128, 1], F32)
    m8a = ctx.sb(es0, "m8a", [128, 1], F32)
    t_a = ctx.sb(es0, "t_a", [128, 1], F32)
    t_b = ctx.sb(es0, "t_b", [128, 1], F32)
    wld = ctx.sb(es0, "wld", [128, 16 * 128], F32)
    wab = ctx.sb(es0, "wab", [128, 16 * 128], BF16)
    wbb = ctx.sb(es0, "wbb", [128, 16 * 128], BF16)
    gx = ctx.sb(es0, "gx", [128, S], F32)
    ga = ctx.sb(es0, "ga", [128, S], F32)
    af = ctx.sb(es0, "af", [128, S], F32)
    e2 = ctx.sb(es0, "e2", [128, S], F32)
    ml = ctx.sb(es0, "ml", [128, S], F32)
    xr = ctx.sb(es0, "xr", [128, S], F32)
    lg = ctx.sb(es0, "lg", [128, S], F32)
    bb = ctx.sb(es0, "bb", [128, S], F32)
    hh = ctx.sb(es0, "hh", [128, S], F32)
    pja = [ctx.ps(es0, f"pja{i}", [128, 512]) for i in range(2)]
    pjb = [ctx.ps(es0, f"pjb{i}", [128, 512]) for i in range(2)]

    @block.sync
    def _(sync):
        ctx.inc("ld", sync.dma_start(out=apv[:], in_=ap_in[:, :]), 16)

    @block.vector
    def _(v):
        ctx.wait(v, "ld")
        ctx.inc("cst", v.memset(onescol[:], 1.0))
        ctx.inc("cst", v.memset(zcol[:], 0.0))

    @block.scalar
    def _(sc):
        ctx.wait(sc, "cst")
        ctx.inc("act", sc.activation(t_a[:], apv[:], AF.Abs))
        ctx.wait(sc, "act")
        ctx.inc("act", sc.activation(t_a[:], t_a[:], AF.Exp, scale=-1.0))
        ctx.wait(sc, "act")
        ctx.inc("act", sc.activation(t_a[:], t_a[:], AF.Ln, bias=onescol[:, 0:1]))
        ctx.wait(sc, "act")
        ctx.inc("act", sc.activation(t_b[:], apv[:], AF.Relu))

    @block.vector
    def _(v):
        ctx.wait(v, "act")
        ctx.inc("cst", v.tensor_tensor(m8a[:], t_a[:], t_b[:], Op.add))
        ctx.wait(v, "cst")
        ctx.inc("cst", v.tensor_scalar(m8a[:], m8a[:], -8.0, None, Op.mult))

    for c in range(16):
        @block.sync
        def _(sync, c=c):
            ctx.wait(sync, "cst")
            ctx.inc("ld", sync.dma_start(out=xld[:], in_=xlT[c * 128:(c + 1) * 128, :]), 16)

        @block.vector
        def _(v, c=c):
            ctx.wait(v, "ld")
            ctx.inc("cst", v.tensor_copy(xbf[:, c * S:(c + 1) * S], xld[:]))

    for fc in range(8):
        @block.sync
        def _(sync, fc=fc):
            ctx.wait(sync, "pe", ctx.val("pe"))
            srca = wia[:, fc * 128:(fc + 1) * 128].rearrange("(c p) n -> p c n", p=128)
            ctx.inc("ld", sync.dma_start(out=wld[:].rearrange("p (c n) -> p c n", c=16), in_=srca), 16)

        @block.vector
        def _(v, fc=fc):
            ctx.wait(v, "ld")
            ctx.inc("cst", v.tensor_copy(wab[:], wld[:]))

        @block.sync
        def _(sync, fc=fc):
            ctx.wait(sync, "cst")
            srcb = wib[:, fc * 128:(fc + 1) * 128].rearrange("(c p) n -> p c n", p=128)
            ctx.inc("ld", sync.dma_start(out=wld[:].rearrange("p (c n) -> p c n", c=16), in_=srcb), 16)

        @block.vector
        def _(v, fc=fc):
            ctx.wait(v, "ld")
            ctx.inc("cst", v.tensor_copy(wbb[:], wld[:]))

        for qt in range(4):
            pa = pja[qt % 2]
            pb_ = pjb[qt % 2]
            @block.tensor
            def _(t, fc=fc, qt=qt, pa=pa, pb_=pb_):
                ctx.wait(t, "cst")
                ctx.wait(t, "act", ctx.val("act"))
                for c in range(16):
                    t.matmul(pa[:], wab[:, c * 128:(c + 1) * 128],
                             xbf[:, c * S + qt * 512: c * S + qt * 512 + 512],
                             start=(c == 0), stop=(c == 15))
                for c in range(16):
                    mm = t.matmul(pb_[:], wbb[:, c * 128:(c + 1) * 128],
                                  xbf[:, c * S + qt * 512: c * S + qt * 512 + 512],
                                  start=(c == 0), stop=(c == 15))
                ctx.inc("pe", mm)

            @block.scalar
            def _(sc, fc=fc, qt=qt, pa=pa, pb_=pb_):
                ctx.wait(sc, "pe")
                ctx.wait(sc, "dve", ctx.val("dve"))
                ctx.inc("act", sc.activation(gx[:, qt * 512:(qt + 1) * 512], pa[:], AF.Sigmoid))
                ctx.wait(sc, "act")
                ctx.inc("act", sc.activation(ga[:, qt * 512:(qt + 1) * 512], pb_[:], AF.Sigmoid))

        @block.vector
        def _(v, fc=fc):
            ctx.wait(v, "act")
            ctx.inc("dve", v.tensor_scalar(lg[:], ga[:], m8a[:, 0:1], None, Op.mult))

        @block.scalar
        def _(sc, fc=fc):
            ctx.wait(sc, "dve")
            ctx.inc("act", sc.activation(af[:], lg[:], AF.Exp))
            ctx.wait(sc, "act")
            ctx.inc("act", sc.activation(e2[:], lg[:], AF.Exp, scale=2.0))
            ctx.wait(sc, "act")
            ctx.inc("act", sc.activation(ml[:], e2[:], AF.Sqrt, scale=-1.0, bias=onescol[:, 0:1]))

        @block.sync
        def _(sync, fc=fc):
            ctx.wait(sync, "dve", ctx.val("dve"))
            ctx.inc("ld", sync.dma_start(out=xr[:], in_=xlT[fc * 128: fc * 128 + 128, :]), 16)

        @block.vector
        def _(v, fc=fc):
            ctx.wait(v, "ld")
            ctx.wait(v, "act")
            ctx.inc("dve", v.tensor_tensor(bb[:], ml[:], gx[:], Op.mult))
            ctx.wait(v, "dve")
            ctx.inc("dve", v.tensor_tensor(bb[:], bb[:], xr[:], Op.mult))
            ctx.wait(v, "dve")
            ctx.inc("dve", v.tensor_tensor_scan(hh[:], af[:], bb[:], zcol[:, 0:1], Op.mult, Op.add))

        @block.gpsimd
        def _(g, fc=fc):
            ctx.wait(g, "dve")
            if fc > 0:
                ctx.wait(g, "sto", ctx.val("sto"))
            ctx.inc("sto", g.dma_start(out=hT[fc * 128:(fc + 1) * 128, :], in_=hh[:]), 16)

    @block.gpsimd
    def _(g):
        ctx.wait(g, "sto")


_L1ATTN_NC = None


def run_l1attn(inputs, xl2):
    global _L1ATTN_NC
    W = np.asarray(inputs["L1_attn_W_in"], dtype=np.float32)
    WT = np.ascontiguousarray(W.T)
    ap = np.asarray(inputs["L1_attn_a_param"], dtype=np.float32).reshape(128, 1)
    in_maps = []
    for c in range(NCORE):
        b, fh = c // 2, c % 2
        xl_full = np.concatenate([xl2[2 * b], xl2[2 * b + 1]], axis=1)
        in_maps.append({
            "xlT": np.ascontiguousarray(np.roll(xl_full, -fh * 1024, axis=0)),
            "wia": np.ascontiguousarray(np.roll(WT, -fh * 1024, axis=0)[:, fh * 1024:(fh + 1) * 1024]),
            "wib": np.ascontiguousarray(np.roll(WT, -fh * 1024, axis=0)[:, 2048 + fh * 1024: 2048 + (fh + 1) * 1024]),
            "ap_in": ap,
        })
    if _L1ATTN_NC is None:
        _L1ATTN_NC = build_l1attn()
    res = run_bass_kernel_spmd(_L1ATTN_NC, in_maps, core_ids=list(range(NCORE)))
    return [res.results[c]["hT"] for c in range(NCORE)]




THR = float(2048.0 / 1025.0)


def build_l1ffn():
    nc = bass.Bass("TRN2", target_bir_lowering=False)
    xl2T = nc.dram_tensor("xl2T", [D, 1024], F32, kind="ExternalInput").ap()
    hTt = nc.dram_tensor("hTt", [D, 1024], F32, kind="ExternalInput").ap()
    wgu = nc.dram_tensor("wgu", [D, 16384], F32, kind="ExternalInput").ap()
    wp = nc.dram_tensor("wp", [8192, D], F32, kind="ExternalInput").ap()
    yT = nc.dram_tensor("yT", [D, 1024], F32, kind="ExternalOutput").ap()
    x3rd = nc.dram_tensor("x3rd", [D, 1024], F32).ap()
    xl3d = nc.dram_tensor("xl3d", [D, 1024], F32).ap()
    scd = nc.dram_tensor("scd", [4, 2048], BF16).ap()

    with contextlib.ExitStack() as es0:
        ctx = Ctx(nc, es0)
        emit_l1ffn(ctx, es0, xl2T, hTt, wgu, wp, yT, x3rd, xl3d, scd)
    return nc


def emit_tquant(ctx, block, q, wl_sl, wb_sl, dst_sl, scd_slot, psc, nch=16):
    nc = ctx.nc
    W = nch * 128

    @block.vector
    def _(v):
        ctx.wait(v, "ld")
        ctx.wait(v, "pe", ctx.val("pe"))
        ctx.inc("cst", v.tensor_copy(wb_sl, wl_sl))

    @block.scalar
    def _(sc):
        ctx.wait(sc, "cst")
        ctx.wait(sc, "pe", ctx.val("pe"))
        ctx.inc("act", sc.activation(q["aw"][:, 0:W], wb_sl, AF.Abs))
        ctx.wait(sc, "act")
        ctx.inc("act", sc.activation(q["sg"][:, 0:W], wb_sl, AF.Sign))

    ng = (nch + 3) // 4
    for g in range(ng):
        @block.tensor
        def _(t, g=g):
            ctx.wait(t, "act", ctx.val("act"))
            ctx.wait(t, "dve", ctx.val("dve"))
            n0 = min(4, nch - g * 4)
            for cc in range(n0):
                c = g * 4 + cc
                mm = t.matmul(psc[:, cc * 128:(cc + 1) * 128], q["g2"][:, :],
                              q["aw"][:, c * 128:(c + 1) * 128],
                              start=True, stop=True)
            ctx.inc("pe", mm)

        @block.vector
        def _(v, g=g):
            ctx.wait(v, "pe")
            n0 = min(4, nch - g * 4)
            ctx.inc("dve", v.tensor_scalar(q["scrow"][:, g * 512:g * 512 + n0 * 128],
                                           psc[:, 0:n0 * 128], 1e-8, None, Op.max))

    @block.gpsimd
    def _(g_):
        ctx.wait(g_, "dve")
        ctx.inc("sto", g_.dma_start(out=scd_slot[:, 0:W], in_=q["scrow"][:, 0:W]), 16)

    @block.sync
    def _(sync):
        ctx.wait(sync, "sto")
        ctx.inc("ld", sync.dma_start(out=q["sb_s"][0:64, 0:W],
                                     in_=scd_slot[0:1, 0:W].broadcast_to([64, W])), 16)
        ctx.inc("ld", sync.dma_start(out=q["sb_s"][64:128, 0:W],
                                     in_=scd_slot[1:2, 0:W].broadcast_to([64, W])), 16)

    @block.vector
    def _(v):
        ctx.wait(v, "ld")
        ctx.wait(v, "act")
        ctx.inc("dve", v.tensor_scalar(q["u"][:, 0:W], q["aw"][:, 0:W], THR, None, Op.mult))
        ctx.wait(v, "dve")
        ctx.inc("dve", v.tensor_tensor(q["g"][:, 0:W], q["u"][:, 0:W], q["sb_s"][:, 0:W], Op.is_gt))
        ctx.wait(v, "dve")
        ctx.inc("dve", v.tensor_tensor(q["t"][:, 0:W], q["sb_s"][:, 0:W], q["g"][:, 0:W], Op.mult))
        ctx.wait(v, "dve")
        ctx.inc("dve", v.tensor_tensor(dst_sl, q["t"][:, 0:W], q["sg"][:, 0:W], Op.mult))


def emit_l1ffn(ctx, es0, xl2T, hTt, wgu, wp, yT, x3rd, xl3d, scd):
    nc = ctx.nc
    block = es0.enter_context(nc.Block())

    XLB3 = ctx.sb(es0, "XLB3", [128, 16 * 1024], BF16)
    FUSEH = ctx.sb(es0, "FUSEH", [128, 64 * 512], BF16)
    onesb = ctx.sb(es0, "onesb", [128, 1], BF16)
    onesf = ctx.sb(es0, "onesf", [1, 128], F32)
    rows = tuple(ctx.sb(es0, f"row{i}", [1, 1024], F32) for i in range(3))
    epsrow = ctx.sb(es0, "epsrow", [1, 1], F32)
    rows = rows + (epsrow,)
    rs3sb = ctx.sb(es0, "rs3sb", [128, 512], F32)
    g2b = ctx.sb(es0, "g2b", [128, 2], BF16)

    @block.vector
    def _(v):
        ctx.inc("cst", v.memset(onesb[:], 1.0))
        ctx.inc("cst", v.memset(onesf[:], 1.0))
        ctx.inc("cst", v.memset(epsrow[:], 1e-5))
        ctx.inc("cst", v.memset(g2b[:, :], 0.0))
        ctx.wait(v, "cst")
        ctx.inc("cst", v.memset(g2b[0:64, 0:1], 1.0 / 64))
        ctx.inc("cst", v.memset(g2b[64:128, 1:2], 1.0 / 64))

    es_a = es0.enter_context(contextlib.ExitStack())
    ldx = ctx.sb(es_a, "ldx", [128, 1024], F32)
    ldp = ctx.sb(es_a, "ldp", [128, 1024], F32)
    tchunk = ctx.sb(es_a, "tchunk", [128, 1024], F32)
    tc2 = ctx.sb(es_a, "tc2", [128, 1024], F32)
    tsq = ctx.sb(es_a, "tsq", [128, 1024], BF16)
    psum_s = ctx.ps(es_a, "psum_s", [1, 1024])
    psum_q = ctx.ps(es_a, "psum_q", [1, 1024])
    pmb = ctx.ps(es_a, "pmb", [128, 1024])
    prb = ctx.ps(es_a, "prb", [128, 1024])

    for c in range(16):
        @block.sync
        def _(sync, c=c):
            ctx.wait(sync, "dve", ctx.val("dve"))
            ctx.wait(sync, "sto", ctx.val("sto"))
            ctx.inc("ld", sync.dma_start(out=ldx[:], in_=xl2T[c * 128:(c + 1) * 128, :]), 16)
            ctx.inc("ld", sync.dma_start(out=ldp[:], in_=hTt[c * 128:(c + 1) * 128, :]), 16)

        @block.vector
        def _(v, c=c):
            ctx.wait(v, "ld")
            ctx.inc("dve", v.tensor_tensor(tchunk[:], ldx[:], ldp[:], Op.add))
            ctx.wait(v, "dve")
            ctx.inc("dve", v.tensor_copy(XLB3[:, c * 1024:(c + 1) * 1024], tchunk[:]))

        @block.gpsimd
        def _(g, c=c):
            ctx.wait(g, "dve")
            ctx.inc("sto", g.dma_start(out=x3rd[c * 128:(c + 1) * 128, :], in_=tchunk[:]), 16)

        @block.scalar
        def _(sc, c=c):
            ctx.wait(sc, "dve")
            ctx.wait(sc, "pe", ctx.val("pe"))
            ctx.inc("act", sc.activation(tsq[:], XLB3[:, c * 1024:(c + 1) * 1024], AF.Square))

        @block.tensor
        def _(t, c=c):
            ctx.wait(t, "act")
            t.matmul(psum_s[:, 0:512], onesb[:], XLB3[:, c * 1024: c * 1024 + 512], start=(c == 0), stop=(c == 15))
            t.matmul(psum_s[:, 512:1024], onesb[:], XLB3[:, c * 1024 + 512: c * 1024 + 1024], start=(c == 0), stop=(c == 15))
            t.matmul(psum_q[:, 0:512], onesb[:], tsq[:, 0:512], start=(c == 0), stop=(c == 15))
            mm = t.matmul(psum_q[:, 512:1024], onesb[:], tsq[:, 512:1024], start=(c == 0), stop=(c == 15))
            ctx.inc("pe", mm)

    emit_ln_calc(ctx, block, psum_s, psum_q, rows, 1e-5)
    emit_ln_bcast(ctx, block, rows, pmb, prb, onesf)

    for c in range(16):
        @block.sync
        def _(sync, c=c):
            ctx.wait(sync, "dve", ctx.val("dve"))
            ctx.wait(sync, "sto", ctx.val("sto"))
            ctx.inc("ld", sync.dma_start(out=ldx[:], in_=x3rd[c * 128:(c + 1) * 128, :]), 16)

        @block.vector
        def _(v, c=c):
            ctx.wait(v, "ld")
            ctx.wait(v, "pe")
            ctx.inc("dve", v.tensor_tensor(tchunk[:], ldx[:], pmb[:], Op.subtract))
            ctx.wait(v, "dve")
            ctx.inc("dve", v.tensor_tensor(tc2[:], tchunk[:], prb[:], Op.mult))
            ctx.wait(v, "dve")
            ctx.inc("dve", v.tensor_copy(XLB3[:, c * 1024:(c + 1) * 1024], tc2[:]))

        @block.gpsimd
        def _(g, c=c):
            ctx.wait(g, "dve")
            ctx.inc("sto", g.dma_start(out=xl3d[c * 128:(c + 1) * 128, :], in_=tc2[:]), 16)

    for engn in ("tensor", "vector", "scalar", "sync", "gpsimd"):
        @getattr(block, engn)
        def _(e):
            ctx.wait(e, "dve")
            ctx.wait(e, "sto")
    es_a.close()

    es_q = es0.enter_context(contextlib.ExitStack())
    qb = {
        "aw": ctx.sb(es_q, "q_aw", [128, 2048], BF16),
        "u": ctx.sb(es_q, "q_u", [128, 2048], F32),
        "g": ctx.sb(es_q, "q_g", [128, 2048], BF16),
        "t": ctx.sb(es_q, "q_t", [128, 2048], BF16),
        "sg": ctx.sb(es_q, "q_sg", [128, 2048], BF16),
        "sb_s": ctx.sb(es_q, "q_sbs", [128, 2048], BF16),
        "scrow": ctx.sb(es_q, "q_scrow", [2, 2048], BF16),
        "g2": g2b,
    }
    wld = ctx.sb(es_q, "wld", [128, 16 * 128], F32)
    wbq = ctx.sb(es_q, "wbq", [128, 64 * 128], BF16)
    wqg = wbq[:, 0:2048]
    wqu = wbq[:, 2048:4096]
    sgt = ctx.sb(es_q, "sgt", [128, 512], BF16)
    tsq4 = ctx.sb(es_q, "tsq4", [128, 512], BF16)
    XN4 = ctx.sb(es_q, "XN4", [128, 16 * 512], F32)
    xl3c = ctx.sb(es_q, "xl3c", [128, 512], F32)
    ty = ctx.sb(es_q, "ty", [128, 512], F32)
    ty2 = xl3c

    for th in range(2):
        toff = th * 512
        es_b = es0.enter_context(contextlib.ExitStack())
        psc = ctx.ps(es_b, f"psc{th}", [2, 512])
        pg_ = ctx.ps(es_b, f"pg_{th}", [128, 512])
        pu_ = ctx.ps(es_b, f"pu_{th}", [128, 512])
        psq3 = ctx.ps(es_b, f"psq3{th}", [1, 512])
        prs3 = ctx.ps(es_b, f"prs3{th}", [128, 512])

        for j in range(64):
            for half, dstq in ((0, wqg), (1, wqu)):
                col = (half * 64 + j) * 128
                @block.sync
                def _(sync, col=col):
                    ctx.wait(sync, "cst", ctx.val("cst"))
                    srcw = wgu[:, col:col + 128].rearrange("(c p) n -> p c n", p=128)
                    ctx.inc("ld", sync.dma_start(
                        out=wld[:].rearrange("p (c n) -> p c n", c=16), in_=srcw), 16)
                emit_tquant(ctx, block, qb, wld[:], wbq[:, 4096:6144],
                            dstq, scd[2 * (j % 2):2 * (j % 2) + 2, :], psc, nch=16)

            @block.tensor
            def _(t, j=j, toff=toff):
                ctx.wait(t, "dve", ctx.val("dve"))
                ctx.wait(t, "act", ctx.val("act"))
                for c in range(16):
                    t.matmul(pg_[:], wqg[:, c * 128:(c + 1) * 128],
                             XLB3[:, c * 1024 + toff: c * 1024 + toff + 512],
                             start=(c == 0), stop=(c == 15))
                for c in range(16):
                    mm = t.matmul(pu_[:], wqu[:, c * 128:(c + 1) * 128],
                                  XLB3[:, c * 1024 + toff: c * 1024 + toff + 512],
                                  start=(c == 0), stop=(c == 15))
                ctx.inc("pe", mm)

            @block.scalar
            def _(sc, j=j):
                ctx.wait(sc, "pe")
                ctx.wait(sc, "dve", ctx.val("dve"))
                ctx.inc("act", sc.activation(sgt[:], pg_[:], AF.Silu))

            @block.vector
            def _(v, j=j):
                ctx.wait(v, "act")
                ctx.inc("dve", v.tensor_tensor(FUSEH[:, j * 512:(j + 1) * 512], sgt[:], pu_[:], Op.mult))

            @block.scalar
            def _(sc, j=j):
                ctx.wait(sc, "dve")
                ctx.wait(sc, "pe", ctx.val("pe"))
                ctx.inc("act", sc.activation(tsq4[:], FUSEH[:, j * 512:(j + 1) * 512], AF.Square))

            @block.tensor
            def _(t, j=j):
                ctx.wait(t, "act", ctx.val("act"))
                mm = t.matmul(psq3[:], onesb[:], tsq4[:], start=(j == 0), stop=(j == 63))
                ctx.inc("pe", mm)

        @block.vector
        def _(v):
            ctx.wait(v, "pe")
            ctx.inc("dve", v.tensor_scalar(rows[0][:, 0:512], psq3[:], 1.0 / 8192.0, 1.1920929e-07, Op.mult, Op.add))

        @block.scalar
        def _(sc):
            ctx.wait(sc, "dve")
            ctx.inc("act", sc.activation(rows[2][:, 0:512], rows[0][:, 0:512], AF.Ln))
            ctx.wait(sc, "act")
            ctx.inc("act", sc.activation(rows[2][:, 0:512], rows[2][:, 0:512], AF.Exp, scale=-0.5))

        @block.tensor
        def _(t):
            ctx.wait(t, "act")
            ctx.inc("pe", t.matmul(prs3[:], onesf[:], rows[2][:, 0:512], start=True, stop=True))

        @block.vector
        def _(v):
            ctx.wait(v, "pe")
            ctx.inc("dve", v.tensor_copy(rs3sb[:], prs3[:]))

        @block.tensor
        def _(t):
            ctx.wait(t, "dve")
        es_b.close()

        es_c = es0.enter_context(contextlib.ExitStack())
        psc2 = ctx.ps(es_c, f"psc2_{th}", [2, 512])
        pdn = [ctx.ps(es_c, f"pdn{th}_{i}", [128, 512]) for i in range(2)]
        psum_s4 = ctx.ps(es_c, f"psum_s4_{th}", [1, 512])
        psum_q4 = ctx.ps(es_c, f"psum_q4_{th}", [1, 512])

        for oc in range(16):
            for sq in range(4):
                @block.sync
                def _(sync, oc=oc, sq=sq):
                    ctx.wait(sync, "cst", ctx.val("cst"))
                    src = wp[sq * 2048:(sq + 1) * 2048, oc * 128:(oc + 1) * 128].rearrange("(c p) n -> p c n", p=128)
                    ctx.inc("ld", sync.dma_start(
                        out=wld[:].rearrange("p (c n) -> p c n", c=16), in_=src), 16)

                emit_tquant(ctx, block, qb, wld[:],
                            wbq[:, sq * 2048:(sq + 1) * 2048],
                            wbq[:, sq * 2048:(sq + 1) * 2048],
                            scd[2 * (sq % 2):2 * (sq % 2) + 2, :], psc2, nch=16)

            pb = pdn[oc % 2]
            @block.tensor
            def _(t, oc=oc, pb=pb):
                ctx.wait(t, "dve", ctx.val("dve"))
                for c in range(64):
                    mm = t.matmul(pb[:], wbq[:, c * 128:(c + 1) * 128],
                                  FUSEH[:, c * 512:(c + 1) * 512],
                                  start=(c == 0), stop=(c == 63))
                ctx.inc("pe", mm)

            @block.sync
            def _(sync, oc=oc, th=th):
                ctx.wait(sync, "dve", ctx.val("dve"))
                ctx.inc("ld", sync.dma_start(out=xl3c[:], in_=xl3d[oc * 128:(oc + 1) * 128, th * 512:th * 512 + 512]), 16)

            @block.vector
            def _(v, oc=oc, pb=pb):
                ctx.wait(v, "pe")
                ctx.wait(v, "ld")
                ctx.inc("dve", v.tensor_tensor(ty[:], pb[:], rs3sb[:], Op.mult))
                ctx.wait(v, "dve")
                ctx.inc("dve", v.tensor_tensor(XN4[:, oc * 512:(oc + 1) * 512], ty[:], xl3c[:], Op.add))
                ctx.wait(v, "dve")
                ctx.inc("dve", v.tensor_copy(tsq4[:], XN4[:, oc * 512:(oc + 1) * 512]))

            @block.scalar
            def _(sc, oc=oc):
                ctx.wait(sc, "dve")
                ctx.wait(sc, "pe", ctx.val("pe"))
                ctx.inc("act", sc.activation(sgt[:], tsq4[:], AF.Square))

            @block.tensor
            def _(t, oc=oc):
                ctx.wait(t, "act")
                t.matmul(psum_s4[:], onesb[:], tsq4[:], start=(oc == 0), stop=(oc == 15))
                mm = t.matmul(psum_q4[:], onesb[:], sgt[:], start=(oc == 0), stop=(oc == 15))
                ctx.inc("pe", mm)

        emit_ln_calc(ctx, block, psum_s4, psum_q4, rows, 1e-5, W=512)

        @block.tensor
        def _(t):
            ctx.wait(t, "act")
        es_c.close()
        es_d = es0.enter_context(contextlib.ExitStack())
        pmb4 = ctx.ps(es_d, f"pmb4_{th}", [128, 512])
        prb4 = ctx.ps(es_d, f"prb4_{th}", [128, 512])
        emit_ln_bcast(ctx, block, rows, pmb4, prb4, onesf, W=512)

        for c in range(16):
            @block.vector
            def _(v, c=c):
                ctx.wait(v, "pe")
                if c > 0:
                    ctx.wait(v, "sto", ctx.val("sto"))
                ctx.inc("dve", v.tensor_tensor(ty[:], XN4[:, c * 512:(c + 1) * 512], pmb4[:], Op.subtract))
                ctx.wait(v, "dve")
                ctx.inc("dve", v.tensor_tensor(ty2[:], ty[:], prb4[:], Op.mult))

            @block.gpsimd
            def _(g, c=c, th=th):
                ctx.wait(g, "dve")
                ctx.inc("sto", g.dma_start(out=yT[c * 128:(c + 1) * 128, th * 512:th * 512 + 512], in_=ty2[:]), 16)

        for engn in ("tensor", "vector", "scalar", "sync", "gpsimd"):
            @getattr(block, engn)
            def _(e):
                ctx.wait(e, "sto")
                ctx.wait(e, "dve")
        es_d.close()

    @block.gpsimd
    def _(g):
        ctx.wait(g, "sto")
    es_q.close()


_L1FFN_NC = None


def run_l1ffn(inputs, xl2, hs):
    global _L1FFN_NC
    wguT = np.ascontiguousarray(np.asarray(inputs["L1_ffn_w_gate_up"], dtype=np.float32).T)
    wpT = np.ascontiguousarray(np.asarray(inputs["L1_ffn_w_proj"], dtype=np.float32).T)
    in_maps = []
    for c in range(NCORE):
        b, sl = c // 2, c % 2
        tsl = slice(sl * 1024, (sl + 1) * 1024)
        h_full = np.concatenate([hs[2 * b], hs[2 * b + 1]], axis=0)
        in_maps.append({
            "xl2T": np.ascontiguousarray(xl2[c]),
            "hTt": np.ascontiguousarray(h_full[:, tsl]),
            "wgu": wguT,
            "wp": wpT,
        })
    if _L1FFN_NC is None:
        _L1FFN_NC = build_l1ffn()
    res = run_bass_kernel_spmd(_L1FFN_NC, in_maps, core_ids=list(range(NCORE)))
    return [res.results[c]["yT"] for c in range(NCORE)]


def kernel(**inputs):
    parts = run_l0attn(inputs)
    xl2 = run_l0ffn(inputs, parts)
    hs = run_l1attn(inputs, xl2)
    ys = run_l1ffn(inputs, xl2, hs)
    out = np.empty((B, S, D), dtype=np.float32)
    for b in range(B):
        out[b, 0:1024] = ys[2 * b].T
        out[b, 1024:2048] = ys[2 * b + 1].T
    return out
